# version 01
# baseline (speedup 1.0000x reference)
import contextlib
import numpy as np
import ml_dtypes
import concourse.bass as bass
import concourse.mybir as mybir

F32 = mybir.dt.float32
BF16 = mybir.dt.bfloat16
AF = mybir.ActivationFunctionType
ALU = mybir.AluOpType
AX = mybir.AxisListType

ENGINES = ("tensor", "vector", "scalar", "gpsimd", "sync")

LN_EPS = 1e-5
QK_SHIFT = 12.0
SC_SHIFT = 40.0


class Sem:
    def __init__(self, handle):
        self.h = handle
        self.n = 0

    def bump(self, k):
        self.n += k
        return self.n


class Prog:

    def __init__(self, nc):
        self.nc = nc
        self.ops = {e: [] for e in ENGINES}
        self.es = contextlib.ExitStack()
        self._semid = 0

    def sem(self, name=None):
        self._semid += 1
        h = self.es.enter_context(
            self.nc.semaphore(name or f"s{self._semid}"))
        return Sem(h)

    def sbuf(self, name, shape, dtype):
        return self.es.enter_context(self.nc.sbuf_tensor(name, shape, dtype))

    def psum(self, name, shape, dtype=F32):
        return self.es.enter_context(self.nc.psum_tensor(name, shape, dtype))

    def wait(self, eng, sem, val=None):
        v = sem.n if val is None else val
        if v <= 0:
            return
        self.ops[eng].append(lambda e, h=sem.h, v=v: e.wait_ge(h, v))

    def emit(self, eng, fn, inc=None, inc_k=1):
        if inc is not None:
            inc.bump(inc_k)
            def wrapped(e, fn=fn, h=inc.h, k=inc_k):
                ins = fn(e)
                ins.then_inc(h, k)
            self.ops[eng].append(wrapped)
        else:
            self.ops[eng].append(fn)

    def dma(self, eng, out_ap, in_ap, sem):
        self.emit(eng, lambda e, o=out_ap, i=in_ap: e.dma_start(o, i),
                  inc=sem, inc_k=16)

    def mk_auto(self):
        self.auto = {e: self.sem(f"auto_{e}") for e in
                     ("tensor", "vector", "scalar", "gpsimd")}
        self.adma = {e: self.sem(f"adma_{e}") for e in ("sync", "gpsimd",
                                                        "scalar")}

    def a(self, eng, fn):
        self.emit(eng, fn, inc=self.auto[eng])

    def admas(self, eng, out_ap, in_ap):
        self.dma(eng, out_ap, in_ap, self.adma[eng])

    def w(self, eng, *srcs):
        for s in srcs:
            if s == "self":
                s = eng
            if s.startswith("d"):
                sem = self.adma[s[1:]]
            else:
                sem = self.auto[s]
            self.wait(eng, sem)

    def finalize(self):
        with self.nc.Block() as block:
            for eng in ENGINES:
                ops = self.ops[eng]
                if not ops:
                    continue
                getattr(block, eng)(lambda h, ops=ops: [f(h) for f in ops])
        self.es.close()



def host_constants():
    c = {}
    c["c_idn"] = np.eye(128, dtype=ml_dtypes.bfloat16)
    sl = np.where(np.tri(128, 128, -1, dtype=bool), 0.0, -1e30)
    il = np.where(np.tri(128, 128, 0, dtype=bool), 0.0, -1e30)
    c["c_mask_sl_T"] = np.ascontiguousarray(sl.T).astype(np.float32)
    c["c_mask_il_T"] = np.ascontiguousarray(il.T).astype(np.float32)
    c["c_mask_su_T"] = np.ascontiguousarray(sl).astype(np.float32)
    c["c_mask_iu_T"] = np.ascontiguousarray(il).astype(np.float32)
    c["c_mask_il_T_bf"] = np.ascontiguousarray(il.T).astype(ml_dtypes.bfloat16)
    c["c_idn_bf"] = np.eye(128, dtype=ml_dtypes.bfloat16)
    c["c_uincl"] = np.triu(np.ones((128, 128), np.float32))
    c["c_ones_row"] = np.ones((1, 128), np.float32)
    c["c_ones_row_bf"] = np.ones((1, 128), dtype=ml_dtypes.bfloat16)
    c["c_idn_f32"] = np.eye(128, dtype=np.float32)
    c["c_ones_col_bf"] = np.ones((128, 1), dtype=ml_dtypes.bfloat16)
    NEG = -1e30
    c["c_m_sl"] = np.where(np.tri(128, 128, -1, dtype=bool), 0., NEG).astype(np.float32)
    c["c_m_su"] = np.where(np.triu(np.ones((128, 128), bool), 1), 0., NEG).astype(np.float32)
    c["c_m_iu"] = np.where(np.triu(np.ones((128, 128), bool), 0), 0., NEG).astype(np.float32)
    for r in range(4):
        k_idx = np.arange(128)[:, None]
        q_idx = np.arange(512)[None, :]
        m = np.where(k_idx <= q_idx - 128 * r, 0., NEG)
        c[f"c_bmask{r}"] = m.astype(ml_dtypes.bfloat16)
    return c


def declare_inputs(nc, specs):
    return {n: nc.dram_tensor(n, list(s), d, kind="ExternalInput")
            for n, (s, d) in specs.items()}


NP2MY = {np.dtype(np.float32): F32, np.dtype(ml_dtypes.bfloat16): BF16}


def input_specs_from_arrays(arrs):
    return {n: (a.shape, NP2MY[a.dtype]) for n, a in arrs.items()}



def emit_ln(p, xt, out_tile, W, tmp, sv, ss, name="", sums=None,
            out_slice=None):
    inv = 1.0 / W
    if sums is None:
        p.emit("vector", lambda v: v.reduce_sum(
            tmp["s1"][:], xt, axis=AX.X), inc=sv)
        sums = tmp["s1"]
    p.wait("scalar", sv)
    p.emit("scalar", lambda s: s.activation(
        tmp["sq"][:, 0:W], xt, AF.Square, accum_out=tmp["ssq"][:]), inc=ss)
    p.wait("vector", sv)
    p.wait("vector", ss)
    p.emit("vector", lambda v: v.tensor_scalar(
        tmp["m"][:], sums[:], inv, None, ALU.mult))
    p.emit("vector", lambda v: v.tensor_tensor(
        tmp["m2"][:], sums[:], sums[:], ALU.mult), inc=sv)
    p.wait("vector", sv)
    p.emit("vector", lambda v: v.scalar_tensor_tensor(
        tmp["var"][:], tmp["m2"][:], -inv, tmp["ssq"][:], ALU.mult, ALU.add),
        inc=sv)
    p.wait("scalar", sv)
    p.emit("scalar", lambda s: s.activation(
        tmp["var"][:], tmp["var"][:], AF.Ln, bias=tmp["weps"][:]), inc=ss)
    p.wait("scalar", ss)
    p.emit("scalar", lambda s: s.activation(
        tmp["rstd"][:], tmp["var"][:], AF.Exp, scale=-0.5,
        bias=tmp["lnwh"][:]), inc=ss)
    p.wait("vector", ss)
    dst = out_tile if out_slice is None else out_slice
    p.emit("vector", lambda v: v.tensor_scalar(
        dst, xt, tmp["m"][:], tmp["rstd"][:], ALU.subtract, ALU.mult), inc=sv)


def make_ln_tmp(p, W, tag=""):
    tmp = {
        "sq": p.sbuf(f"ln_sq{tag}", [128, W], F32),
        "s1": p.sbuf(f"ln_s1{tag}", [128, 1], F32),
        "ssq": p.sbuf(f"ln_ssq{tag}", [128, 1], F32),
        "m": p.sbuf(f"ln_m{tag}", [128, 1], F32),
        "m2": p.sbuf(f"ln_m2{tag}", [128, 1], F32),
        "var": p.sbuf(f"ln_var{tag}", [128, 1], F32),
        "rstd": p.sbuf(f"ln_rstd{tag}", [128, 1], F32),
        "weps": p.sbuf(f"ln_weps{tag}", [128, 1], F32),
        "lnwh": p.sbuf(f"ln_lnwh{tag}", [128, 1], F32),
        "W": W,
    }
    return tmp


def emit_eps_init(p, tmp, sv):
    W = tmp["W"]
    p.emit("vector", lambda v: v.memset(tmp["weps"][:], W * LN_EPS))
    p.emit("vector", lambda v: v.memset(tmp["lnwh"][:], 0.5 * float(np.log(W))),
           inc=sv)



def build_ln_tr(nc, ntok=1024, hid=2048):
    ins = declare_inputs(nc, {
        "xin": ([ntok, hid], F32),
        "c_idn": ([128, 128], BF16),
    })
    xout = nc.dram_tensor("xlnt", [hid, ntok], BF16, kind="ExternalOutput")
    p = Prog(nc)
    ntt = ntok // 128
    nh = hid // 128

    xt = [p.sbuf(f"xt{i}", [128, hid], F32) for i in range(2)]
    xln = p.sbuf("xln", [128, hid], BF16)
    idn = p.sbuf("idn", [128, 128], BF16)
    strips = [p.sbuf(f"strip{h}", [128, ntok], BF16) for h in range(nh)]
    ptr = [p.psum(f"ptr{i}", [128, 128], BF16) for i in range(4)]
    tmp = make_ln_tmp(p, hid)

    sl = p.sem("sl")
    sv = p.sem("sv")
    ss = p.sem("ss")
    st = p.sem("st")
    se = p.sem("se")
    so = p.sem("so")

    emit_eps_init(p, tmp, sv)
    p.dma("sync", idn[:], ins["c_idn"][:, :], sl)

    tr_done = 0
    ev_done = 0
    for t in range(ntt):
        buf = xt[t % 2]
        if t >= 2:
            pass
        p.wait("sync", sv)
        p.wait("sync", ss)
        p.dma("sync", buf[:], ins["xin"][t * 128:(t + 1) * 128, :], sl)
        p.wait("vector", sl)
        p.wait("scalar", sl)
        if t > 0:
            p.wait("vector", st, tr_done)
        emit_ln(p, buf[:], xln[:], hid, tmp, sv, ss, name=f"t{t}")
        p.wait("tensor", sv)
        if t == 0:
            p.wait("tensor", sl, 16)
        for h in range(nh):
            bank = (tr_done) % 4
            if tr_done >= 4:
                p.wait("tensor", se, (tr_done - 4 + 1))
            p.emit("tensor", lambda tt, b=bank, h=h: tt.transpose(
                ptr[b][:], xln[:, h * 128:(h + 1) * 128], idn[:]), inc=st)
            tr_done += 1
            p.wait("scalar", st, tr_done)
            p.emit("scalar", lambda s, b=bank, h=h, t=t: s.copy(
                strips[h][:, t * 128:(t + 1) * 128], ptr[b][:]), inc=se)
            ev_done += 1
    p.wait("sync", se, ev_done)
    for h in range(nh):
        p.dma("sync", xout[h * 128:(h + 1) * 128, :], strips[h][:], so)
    p.wait("sync", so, so.n)
    p.finalize()
    return nc



DSCALE = float(1.0 / np.sqrt(128.0))


def build_l0_attn(nc, nkv=4, do_base=True, do_delta=True, stages=('proj','vt','ek','lnrows','g')):
    SEQ, HID = 2048, 2048
    NKV, NQH = 4, 8
    ins = declare_inputs(nc, {
        "xlnt": ([HID, SEQ], BF16),
        "wq": ([HID, 1024], F32),
        "wk": ([HID, 512], F32),
        "wv": ([HID, 512], F32),
        "wg": ([128, 1], F32),
        "bg": ([1, 1], F32),
        "c_idn": ([128, 128], BF16),
        "c_idn_f32": ([128, 128], F32),
        "c_uincl": ([128, 128], F32),
        "c_ones_row": ([1, 128], F32),
        "c_ones_col_bf": ([128, 1], BF16),
        "c_m_sl": ([128, 128], F32),
        "c_m_su": ([128, 128], F32),
        "c_m_iu": ([128, 128], F32),
        "c_bmask0": ([128, 512], BF16),
        "c_bmask1": ([128, 512], BF16),
        "c_bmask2": ([128, 512], BF16),
        "c_bmask3": ([128, 512], BF16),
    })
    fla_o = nc.dram_tensor("fla", [SEQ, 1024], BF16, kind="ExternalOutput")
    base_o = nc.dram_tensor("baseo", [1024, SEQ], BF16, kind="ExternalOutput")

    p = Prog(nc)
    p.mk_auto()

    xl = p.sbuf("xl", [128, 16 * SEQ], BF16)
    wq_sb = [p.sbuf(f"wq{i}", [128, 16 * 128], BF16) for i in range(2)]
    wk_sb = p.sbuf("wk_sb", [128, 16 * 128], BF16)
    wv_sb = p.sbuf("wv_sb", [128, 16 * 128], BF16)
    wg_bf = p.sbuf("wg_bf", [128, 1], BF16)
    bgb = p.sbuf("bgb", [128, 1], F32)
    k_fm = p.sbuf("k_fm", [128, SEQ], BF16)
    v_fm = p.sbuf("v_fm", [128, SEQ], BF16)
    v_tm = p.sbuf("v_tm", [128, SEQ], BF16)
    q_fm = [p.sbuf(f"q_fm{i}", [128, SEQ], BF16) for i in range(2)]
    eq_fm = [p.sbuf(f"eq_fm{i}", [128, SEQ], BF16) for i in range(2)]
    ek_fm = p.sbuf("ek_fm", [128, SEQ], BF16)
    ek_tm = p.sbuf("ek_tm", [128, SEQ], BF16)
    lnk = p.sbuf("lnk", [1, SEQ], F32)
    lnq = [p.sbuf(f"lnq{i}", [1, SEQ], F32) for i in range(2)]
    g_sb = p.sbuf("g_sb", [128, 16], F32)
    gt1 = p.sbuf("gt1", [128, 16], F32)
    gt2 = p.sbuf("gt2", [128, 16], F32)
    gt3 = p.sbuf("gt3", [128, 16], F32)
    gs_sb = p.sbuf("gs_sb", [128, 16], F32)
    ptile = [p.sbuf(f"ptile{i}", [128, 512], BF16) for i in range(2)]
    rrow = p.sbuf("rrow", [1, 512], F32)
    rbc = p.sbuf("rbc", [128, 512], F32)
    obase = p.sbuf("obase", [128, SEQ], BF16)
    fla_sb = p.sbuf("fla_sb", [128, 16 * 1024], BF16)
    arow = p.sbuf("arow", [1, 128], F32)
    bk_r = p.sbuf("bk_r", [1, 128], F32)
    nck_r = p.sbuf("nck_r", [1, 128], F32)
    bq_r = [p.sbuf(f"bq_r{i}", [1, 128], F32) for i in range(2)]
    cols_sb = p.sbuf("cols_sb", [128, 6], F32)
    negb = p.sbuf("negb", [128, 1], F32)
    kscale = p.sbuf("kscale", [128, 1], F32)
    bqe = [p.sbuf(f"bqe{i}", [128, 1], F32) for i in range(2)]
    ace = p.sbuf("ace", [128, 1], F32)
    e_sb = p.sbuf("e_sb", [128, 512], F32)
    t_bf = p.sbuf("t_bf", [128, 128], BF16)
    tt_bf = p.sbuf("tt_bf", [128, 128], BF16)
    pt_bf = [p.sbuf(f"pt_bf{i}", [128, 128], BF16) for i in range(2)]
    pow_sb = p.sbuf("pow_sb", [128, 5 * 128], BF16)
    powt_sb = p.sbuf("powt_sb", [128, 6 * 128], BF16)
    xbuf = [p.sbuf(f"xbuf{i}", [128, 128], BF16) for i in range(2)]
    u_bf = p.sbuf("u_bf", [128, 128], BF16)
    r_bf = p.sbuf("r_bf", [128, 128], BF16)
    kt_bf = p.sbuf("kt_bf", [128, 128], BF16)
    s_f32 = [p.sbuf(f"s_f32{i}", [128, 128], F32) for i in range(2)]
    s_bf = p.sbuf("s_bf", [128, 128], BF16)
    zt_sb = p.sbuf("zt_sb", [128, 128], F32)
    neg12 = p.sbuf("neg12", [128, 1], F32)
    neg40 = p.sbuf("neg40", [128, 1], F32)
    one_t = p.sbuf("one_t", [128, 1], F32)
    idn_bf = p.sbuf("idn_bf", [128, 128], BF16)
    idn_f = p.sbuf("idn_f", [128, 128], F32)
    uincl = p.sbuf("uincl_s", [128, 128], F32)
    ones_r = p.sbuf("ones_r", [1, 128], F32)
    ones_cb = p.sbuf("ones_cb", [128, 1], BF16)
    m_sl = p.sbuf("m_sl", [128, 128], F32)
    m_su = p.sbuf("m_su", [128, 128], F32)
    m_iu = p.sbuf("m_iu", [128, 128], F32)
    bmask = [p.sbuf(f"bmask{r}", [128, 512], BF16) for r in range(4)]

    PS = [p.psum(f"PS{i}", [128, 512], F32) for i in range(6)]
    PTB = [p.psum(f"PTB{i}", [128, 128], BF16) for i in range(2)]

    T, V, S, G, SY = "tensor", "vector", "scalar", "gpsimd", "sync"

    for nm, dst in [("c_idn", idn_bf), ("c_idn_f32", idn_f), ("c_uincl", uincl),
                    ("c_ones_row", ones_r), ("c_ones_col_bf", ones_cb),
                    ("c_m_sl", m_sl), ("c_m_su", m_su), ("c_m_iu", m_iu)]:
        p.admas(SY, dst[:], ins[nm][:, :])
    for r in range(4):
        p.admas(SY, bmask[r][:], ins[f"c_bmask{r}"][:, :])
    for c in range(16):
        p.admas(SY, xl[:, c * SEQ:(c + 1) * SEQ], ins["xlnt"][c * 128:(c + 1) * 128, :])
    p.admas(G, wg_bf[:], ins["wg"][:, :])
    p.admas(SY, bgb[:], ins["bg"][:, :].broadcast_to([128, 1]))
    p.a(V, lambda v: v.memset(neg12[:], -QK_SHIFT))
    p.a(V, lambda v: v.memset(neg40[:], -SC_SHIFT))
    p.a(V, lambda v: v.memset(one_t[:], 1.0))

    def mm(out_ap, l_ap, r_ap, start=True, stop=True):
        p.a(T, lambda t, o=out_ap, a=l_ap, b=r_ap, st=start, sp=stop:
            t.matmul(o, a, b, start=st, stop=sp))

    def scp(out_ap, in_ap):
        p.a(S, lambda s, o=out_ap, i=in_ap: s.copy(o, i))

    def act(out_ap, in_ap, f, bias=0.0, scale=1.0, accum=None):
        if accum is None:
            p.a(S, lambda s, o=out_ap, i=in_ap, ff=f, b=bias, sc=scale:
                s.activation(o, i, ff, bias=b, scale=sc))
        else:
            p.a(S, lambda s, o=out_ap, i=in_ap, ff=f, b=bias, sc=scale, ac=accum:
                s.activation(o, i, ff, bias=b, scale=sc, accum_out=ac))

    def proj(dst_sb, w_sb, wofs, nblocks=4):
        for n in range(nblocks):
            p.w(T, "dsync", "dgpsimd", "scalar")
            for c in range(16):
                mm(PS[0][:, 0:512], w_sb[:, wofs + c * 128: wofs + (c + 1) * 128],
                   xl[:, c * SEQ + n * 512: c * SEQ + n * 512 + 512],
                   start=(c == 0), stop=(c == 15))
            p.w(S, "tensor")
            scp(dst_sb[:, n * 512:(n + 1) * 512], PS[0][:, 0:512])

    first_w = [True]

    for j in range(nkv):
        h0, h1 = 2 * j, 2 * j + 1
        p.w(G, "tensor")
        p.admas(G, wk_sb[:].rearrange("p (c j) -> p c j", c=16),
                ins["wk"][:, j * 128:(j + 1) * 128]
                .rearrange("(c p) j -> p c j", p=128))
        p.admas(G, wv_sb[:].rearrange("p (c j) -> p c j", c=16),
                ins["wv"][:, j * 128:(j + 1) * 128]
                .rearrange("(c p) j -> p c j", p=128))
        for dh in range(2):
            p.admas(G, wq_sb[dh][:].rearrange("p (c j) -> p c j", c=16),
                    ins["wq"][:, (2 * j + dh) * 128:(2 * j + dh + 1) * 128]
                    .rearrange("(c p) j -> p c j", p=128))
        p.w(T, "dgpsimd", "dsync")
        if 'proj' in stages:
            proj(k_fm, wk_sb, 0)
            proj(v_fm, wv_sb, 0)
            proj(q_fm[0], wq_sb[0], 0)
            proj(q_fm[1], wq_sb[1], 0)
        for c in range(16 if 'vt' in stages else 0):
            p.w(T, "scalar")
            p.a(T, lambda t, c=c: t.transpose(
                PTB[c % 2][:], v_fm[:, c * 128:(c + 1) * 128], idn_bf[:]))
            p.w(S, "tensor")
            scp(v_tm[:, c * 128:(c + 1) * 128], PTB[c % 2][:])
        p.w(S, "tensor", "vector")
        if 'ek' in stages:
            act(ek_fm[:], k_fm[:], AF.Exp, bias=neg12[:])
            act(eq_fm[0][:], q_fm[0][:], AF.Exp, bias=neg12[:])
            act(eq_fm[1][:], q_fm[1][:], AF.Exp, bias=neg12[:])
        for c in range(16 if 'ek' in stages else 0):
            p.w(T, "scalar")
            p.a(T, lambda t, c=c: t.transpose(
                PTB[c % 2][:], ek_fm[:, c * 128:(c + 1) * 128], idn_bf[:]))
            p.w(S, "tensor")
            scp(ek_tm[:, c * 128:(c + 1) * 128], PTB[c % 2][:])
        for (src, dstrow) in ([(ek_fm, lnk), (eq_fm[0], lnq[0]), (eq_fm[1], lnq[1])] if 'lnrows' in stages else []):
            for n in range(4):
                p.w(T, "scalar")
                mm(PS[2][0:1, 0:512], ones_cb[:, 0:1], src[:, n * 512:(n + 1) * 512])
                p.w(S, "tensor")
                act(dstrow[0:1, n * 512:(n + 1) * 512], PS[2][0:1, 0:512], AF.Ln)
        p.w(T, "scalar", "dgpsimd")
        for c in range(16 if 'g' in stages else 0):
            mm(PS[2][:, c:c + 1], k_fm[:, c * 128:(c + 1) * 128], wg_bf[:, 0:1])
        p.w(V, "tensor", "dsync")
        if 'g' not in stages:
            pass
        else:
            p.a(V, lambda v: v.tensor_scalar(g_sb[:], PS[2][:, 0:16], bgb[:], None, ALU.add))
        if 'g' in stages:
            p.w(S, "vector")
            act(gt1[:], g_sb[:], AF.Abs)
            p.w(S, "self")
            act(gt2[:], gt1[:], AF.Exp, scale=-1.0)
            p.w(S, "self")
            act(gt3[:], gt2[:], AF.Ln, bias=one_t[:])
            p.w(V, "scalar")
            p.a(V, lambda v: v.tensor_scalar_min(gt1[:], g_sb[:], 0.0))
            p.w(V, "self")
            p.a(V, lambda v: v.tensor_sub(gt2[:], gt1[:], gt3[:]))
            p.w(V, "self")
            p.a(V, lambda v: v.tensor_scalar_mul(gs_sb[:], gt2[:], 1.0 / 16.0))

        for dh in range(2 if do_base else 0):
            hl = 2 * j + dh
            p.w(V, "dsync")
            for qb in range(4):
                nkc = (qb + 1) * 4
                for kc in range(nkc):
                    sb = 4 + (kc % 2)
                    p.w(T, "scalar", "vector")
                    mm(PS[sb][:, 0:512], k_fm[:, kc * 128:(kc + 1) * 128],
                       q_fm[dh][:, qb * 512:(qb + 1) * 512],
                       start=True, stop=(kc < qb * 4))
                    if kc >= qb * 4:
                        mm(PS[sb][:, 0:512], idn_bf[:], bmask[kc - qb * 4][:],
                           start=False, stop=True)
                    p.w(S, "tensor")
                    act(ptile[kc % 2][:], PS[sb][:, 0:512], AF.Exp, bias=neg40[:], scale=DSCALE)
                    p.w(T, "scalar")
                    mm(PS[2][0:1, 0:512], ones_cb[:, 0:1], ptile[kc % 2][:],
                       start=(kc == 0), stop=(kc == nkc - 1))
                    mm(PS[3][:, 0:512], v_tm[:, kc * 128:(kc + 1) * 128],
                       ptile[kc % 2][:], start=(kc == 0), stop=(kc == nkc - 1))
                p.w(V, "tensor")
                p.a(V, lambda v: v.reciprocal(rrow[0:1, 0:512], PS[2][0:1, 0:512]))
                p.w(T, "vector")
                mm(PS[1][:, 0:512], ones_r[0:1, :], rrow[0:1, 0:512])
                p.w(S, "tensor")
                scp(rbc[:], PS[1][:, 0:512])
                p.w(V, "scalar")
                p.a(V, lambda v, qb=qb: v.tensor_tensor(
                    obase[:, qb * 512:(qb + 1) * 512], PS[3][:, 0:512], rbc[:],
                    ALU.mult))
            p.w(SY, "vector")
            p.admas(SY, base_o[hl * 128:(hl + 1) * 128, :], obase[:])

        for c in range(16 if do_delta else 0):
            cs = slice(c * 128, (c + 1) * 128)
            p.w(T, "vector", "scalar")
            mm(PS[5][0:1, 128:256], gs_sb[:, c:c + 1], uincl[:])
            p.w(S, "tensor")
            scp(arow[0:1, :], PS[5][0:1, 128:256])
            p.w(V, "scalar")
            p.a(V, lambda v, cs=cs: v.tensor_sub(bk_r[0:1, :], arow[0:1, :], lnk[0:1, cs]))
            p.a(V, lambda v, cs=cs: v.scalar_tensor_tensor(
                nck_r[0:1, :], arow[0:1, :], -1.0, lnk[0:1, cs], ALU.mult, ALU.subtract))
            p.a(V, lambda v, cs=cs: v.tensor_sub(bq_r[0][0:1, :], arow[0:1, :], lnq[0][0:1, cs]))
            p.a(V, lambda v, cs=cs: v.tensor_sub(bq_r[1][0:1, :], arow[0:1, :], lnq[1][0:1, cs]))
            p.w(T, "vector")
            mm(PS[5][:, 0:1], bk_r[0:1, :], ones_r[0:1, 0:1])
            mm(PS[5][:, 1:2], nck_r[0:1, :], ones_r[0:1, 0:1])
            mm(PS[5][:, 2:3], bq_r[0][0:1, :], ones_r[0:1, 0:1])
            mm(PS[5][:, 3:4], bq_r[1][0:1, :], ones_r[0:1, 0:1])
            mm(PS[5][:, 4:5], ones_r[0:1, :], arow[0:1, 127:128])
            p.w(S, "tensor")
            scp(cols_sb[:, 0:5], PS[5][:, 0:5])
            p.w(S, "self")
            act(negb[:], cols_sb[:, 0:1], AF.Exp)
            p.w(V, "scalar")
            p.a(V, lambda v: v.tensor_scalar_mul(negb[:], negb[:], -1.0))
            p.a(V, lambda v: v.tensor_add(cols_sb[:, 5:6], cols_sb[:, 4:5], cols_sb[:, 1:2]))
            p.w(S, "vector")
            act(kscale[:], cols_sb[:, 5:6], AF.Exp)
            act(bqe[0][:], cols_sb[:, 2:3], AF.Exp)
            act(bqe[1][:], cols_sb[:, 3:4], AF.Exp)
            act(ace[:], cols_sb[:, 4:5], AF.Exp)
            p.w(T, "scalar")
            mm(PS[0][:, 0:128], bk_r[0:1, :], ones_r[0:1, :], start=True, stop=False)
            mm(PS[0][:, 0:128], ones_r[0:1, :], nck_r[0:1, :], start=False, stop=False)
            mm(PS[0][:, 0:128], idn_f[:], m_sl[:], start=False, stop=True)
            mm(PS[0][:, 128:256], nck_r[0:1, :], ones_r[0:1, :], start=True, stop=False)
            mm(PS[0][:, 128:256], ones_r[0:1, :], bq_r[0][0:1, :], start=False, stop=False)
            mm(PS[0][:, 128:256], idn_f[:], m_iu[:], start=False, stop=True)
            mm(PS[0][:, 256:384], nck_r[0:1, :], ones_r[0:1, :], start=True, stop=False)
            mm(PS[0][:, 256:384], ones_r[0:1, :], bq_r[1][0:1, :], start=False, stop=False)
            mm(PS[0][:, 256:384], idn_f[:], m_iu[:], start=False, stop=True)
            mm(PS[0][:, 384:512], nck_r[0:1, :], ones_r[0:1, :], start=True, stop=False)
            mm(PS[0][:, 384:512], ones_r[0:1, :], bk_r[0:1, :], start=False, stop=False)
            mm(PS[0][:, 384:512], idn_f[:], m_su[:], start=False, stop=True)
            p.w(S, "tensor")
            act(e_sb[:], PS[0][:, 0:512], AF.Exp)
            p.w(T, "scalar")
            mm(PS[1][:, 0:128], ek_fm[:, cs], ek_fm[:, cs])
            mm(PS[1][:, 128:256], ek_fm[:, cs], eq_fm[0][:, cs])
            mm(PS[1][:, 256:384], ek_fm[:, cs], eq_fm[1][:, cs])
            p.w(V, "tensor", "scalar")
            p.a(V, lambda v: v.tensor_tensor(t_bf[:], e_sb[:, 0:128], PS[1][:, 0:128], ALU.mult))
            p.a(V, lambda v: v.tensor_tensor(pt_bf[0][:], e_sb[:, 128:256], PS[1][:, 128:256], ALU.mult))
            p.a(V, lambda v: v.tensor_tensor(pt_bf[1][:], e_sb[:, 256:384], PS[1][:, 256:384], ALU.mult))
            p.a(V, lambda v: v.tensor_tensor(tt_bf[:], e_sb[:, 384:512], PS[1][:, 0:128], ALU.mult))
            p.w(T, "vector")
            prevX, prevXt = t_bf, tt_bf
            for lev in range(1, 6):
                mm(PS[2][:, 0:128], prevXt[:], prevX[:])
                mm(PS[2][:, 128:256], prevX[:], prevXt[:])
                p.w(S, "tensor")
                scp(pow_sb[:, (lev - 1) * 128:lev * 128], PS[2][:, 0:128])
                scp(powt_sb[:, (lev - 1) * 128:lev * 128], PS[2][:, 128:256])
                p.w(T, "scalar")
                prevX = pow_sb[:, (lev - 1) * 128:lev * 128]
                prevXt = powt_sb[:, (lev - 1) * 128:lev * 128]
            mm(PS[2][:, 0:128], pow_sb[:, 4 * 128:5 * 128], powt_sb[:, 4 * 128:5 * 128])
            p.w(S, "tensor")
            scp(powt_sb[:, 5 * 128:6 * 128], PS[2][:, 0:128])
            if c > 0:
                p.w(T, "scalar", "vector")
                mm(PS[3][:, 0:128], ek_fm[:, cs], s_bf[:])
                p.w(V, "tensor")
                p.a(V, lambda v, cs=cs: v.scalar_tensor_tensor(
                    r_bf[:], PS[3][:, 0:128], negb[:], v_tm[:, cs], ALU.mult, ALU.add))
                X = r_bf
            else:
                X = None
            levs = [5, 4, 3, 2, 1, 0]
            cur = (v_tm[:, cs] if X is None else r_bf[:])
            xb = 0
            for li, lev in enumerate([5, 4, 3, 2, 1]):
                p.w(T, "scalar", "vector")
                mm(PS[3][:, 128:256], powt_sb[:, lev * 128:(lev + 1) * 128], cur)
                p.w(V, "tensor")
                p.a(V, lambda v, xb=xb, cur=cur: v.tensor_tensor(
                    xbuf[xb][:], cur, PS[3][:, 128:256], ALU.add))
                cur = xbuf[xb][:]
                xb = 1 - xb
            p.w(T, "vector")
            mm(PS[3][:, 128:256], powt_sb[:, 0:128], cur)
            p.w(V, "tensor")
            p.a(V, lambda v, xb=xb, cur=cur: v.tensor_tensor(
                xbuf[xb][:], cur, PS[3][:, 128:256], ALU.add))
            cur = xbuf[xb][:]
            p.w(T, "vector")
            mm(PS[3][:, 128:256], tt_bf[:], cur)
            p.w(V, "tensor")
            p.a(V, lambda v, cur=cur: v.tensor_sub(u_bf[:], cur, PS[3][:, 128:256]))
            p.w(T, "vector")
            if c > 0:
                mm(PS[3][:, 256:384], eq_fm[0][:, cs], s_bf[:])
                mm(PS[3][:, 384:512], eq_fm[1][:, cs], s_bf[:])
            mm(PS[4][:, 0:128], pt_bf[0][:], u_bf[:])
            mm(PS[4][:, 128:256], pt_bf[1][:], u_bf[:])
            p.w(V, "tensor")
            for dh in range(2):
                hl = 2 * j + dh
                dst = fla_sb[:, c * 1024 + hl * 128: c * 1024 + hl * 128 + 128]
                if c > 0:
                    p.a(V, lambda v, dh=dh: v.tensor_scalar(
                        zt_sb[:], PS[3][:, 256 + dh * 128:384 + dh * 128],
                        bqe[dh][:], DSCALE, ALU.mult, ALU.mult))
                    p.w(V, "self")
                    p.a(V, lambda v, dh=dh, dst=dst: v.scalar_tensor_tensor(
                        dst, PS[4][:, dh * 128:(dh + 1) * 128], DSCALE, zt_sb[:],
                        ALU.mult, ALU.add))
                else:
                    p.a(V, lambda v, dh=dh, dst=dst: v.tensor_scalar(
                        dst, PS[4][:, dh * 128:(dh + 1) * 128], DSCALE, None,
                        ALU.mult))
            if c < 15:
                p.w(V, "scalar")
                p.a(V, lambda v, cs=cs: v.tensor_scalar(
                    kt_bf[:], ek_tm[:, cs], kscale[:], None, ALU.mult))
                p.w(T, "vector")
                mm(PS[5][:, 128:256], kt_bf[:], u_bf[:])
                p.w(V, "tensor")
                cu = c % 2
                if c == 0:
                    p.a(V, lambda v, cu=cu: v.tensor_copy(s_f32[cu][:], PS[5][:, 128:256]))
                else:
                    p.a(V, lambda v, cu=cu: v.scalar_tensor_tensor(
                        s_f32[cu][:], s_f32[1 - cu][:], ace[:], PS[5][:, 128:256],
                        ALU.mult, ALU.add))
                p.w(S, "vector")
                scp(s_bf[:], s_f32[cu][:])
    p.w(SY, "vector")
    for c in range(16):
        p.admas(SY, fla_o[c * 128:(c + 1) * 128, :], fla_sb[:, c * 1024:(c + 1) * 1024])
    p.wait(SY, p.adma[SY], p.adma[SY].n)
    p.finalize()
    return nc



def build_mix(nc, mode):
    NT, HID = 1024, 2048
    specs = {"xres": ([NT, HID], F32), "wo": ([HID, HID], F32)}
    if mode == "l0":
        specs["fla"] = ([NT, HID], BF16)
        specs["base"] = ([HID, NT], BF16)
    else:
        specs["ctx"] = ([HID, NT], BF16)
        specs["bo"] = ([1, HID], F32)
    ins = declare_inputs(nc, specs)
    xout = nc.dram_tensor("xout", [NT, HID], F32, kind="ExternalOutput")

    p = Prog(nc)
    p.mk_auto()
    wo_sb = p.sbuf("wo_sb", [128, 16 * HID], BF16)
    bs_sb = p.sbuf("bs_sb", [128, 16 * NT], BF16)
    xres_t = [p.sbuf(f"xres{i}", [128, HID], F32) for i in range(2)]
    fla_t = [p.sbuf(f"fla{i}", [128, HID], BF16) for i in range(2)]
    bo_bc = p.sbuf("bo_bc", [128, HID], F32)
    fs = p.sbuf("fs", [128, 512], F32)
    mixed = p.sbuf("mixed", [128, HID], F32)
    lnout = p.sbuf("lnout", [128, HID], F32)
    xo = p.sbuf("xo", [128, HID], F32)
    tmp = make_ln_tmp(p, HID)
    PS = [p.psum(f"PS{i}", [128, 512], F32) for i in range(2)]
    T, V, S, G, SY = "tensor", "vector", "scalar", "gpsimd", "sync"

    p.a(V, lambda v: v.memset(tmp["weps"][:], HID * LN_EPS))
    p.a(V, lambda v: v.memset(tmp["lnwh"][:], 0.5 * float(np.log(HID))))
    for c in range(4):
        p.admas(G, wo_sb[:].rearrange("p (c j) -> p c j", c=16)[:, 4 * c:4 * c + 4, :],
                ins["wo"][:, :].rearrange("(c p) j -> p c j", p=128)[:, 4 * c:4 * c + 4, :])
    srcb = ins["base"] if mode == "l0" else ins["ctx"]
    for c in range(16):
        p.admas(SY, bs_sb[:, c * NT:(c + 1) * NT], srcb[c * 128:(c + 1) * 128, :])
    if mode == "l1":
        p.admas(SY, bo_bc[:], ins["bo"][:, :].broadcast_to([128, HID]))

    for t in range(8):
        bi = t % 2
        p.w(SY, "vector")
        p.admas(SY, xres_t[bi][:], ins["xres"][t * 128:(t + 1) * 128, :])
        if mode == "l0":
            p.admas(SY, fla_t[bi][:], ins["fla"][t * 128:(t + 1) * 128, :])
        for n in range(4):
            p.w(T, "vector", "dsync", "dgpsimd")
            p.w(V, "scalar")
            for c in range(16):
                mm_l = bs_sb[:, c * NT + t * 128: c * NT + (t + 1) * 128]
                mm_r = wo_sb[:, c * HID + n * 512: c * HID + n * 512 + 512]
                p.a(T, lambda tt, o=PS[0][:, 0:512], a=mm_l, b=mm_r, st=(c == 0),
                    sp=(c == 15): tt.matmul(o, a, b, start=st, stop=sp))
            p.w(V, "tensor", "dsync")
            if mode == "l0":
                p.a(V, lambda v, bi=bi, n=n: v.tensor_scalar(
                    fs[:], fla_t[bi][:, n * 512:(n + 1) * 512], 0.01, None, ALU.mult))
                p.w(V, "self")
                p.a(V, lambda v, n=n: v.scalar_tensor_tensor(
                    mixed[:, n * 512:(n + 1) * 512], PS[0][:, 0:512], 0.99, fs[:],
                    ALU.mult, ALU.add))
            else:
                p.a(V, lambda v, n=n: v.tensor_tensor(
                    mixed[:, n * 512:(n + 1) * 512], PS[0][:, 0:512],
                    bo_bc[:, n * 512:(n + 1) * 512], ALU.add))
        p.w(V, "self")
        p.w(S, "vector")
        emit_ln(p, mixed[:], lnout[:], HID, tmp, p.auto["vector"], p.auto["scalar"])
        p.w(V, "self")
        p.w(V, "dsync")
        p.a(V, lambda v, bi=bi: v.tensor_add(xo[:], xres_t[bi][:], lnout[:]))
        p.w(SY, "vector")
        p.admas(SY, xout[t * 128:(t + 1) * 128, :], xo[:])
    p.wait(SY, p.adma[SY], p.adma[SY].n)
    p.finalize()
    return nc



def build_moe(nc, nexp=16):
    NT, HID = 1024, 2048
    E, I, SI = 16, 1408, 5632
    ins = declare_inputs(nc, {
        "xres": ([NT, HID], F32),
        "gw_t": ([HID, 16], F32),
        "egu_t": ([E, HID, 2 * I], F32),
        "edn_t": ([E, I, HID], F32),
        "sgu_t": ([HID, 2 * SI], F32),
        "sdn_t": ([SI, HID], F32),
        "sgv_l": ([128, 16], F32),
        "c_idn_f32": ([128, 128], F32),
    })
    xout = nc.dram_tensor("xout", [NT, HID], F32, kind="ExternalOutput")
    p = Prog(nc)
    p.mk_auto()
    T, V, S, G, SY = "tensor", "vector", "scalar", "gpsimd", "sync"

    xrt = [p.sbuf(f"xrt{i}", [128, HID], F32) for i in range(2)]
    xlnf = p.sbuf("xlnf", [128, HID], F32)
    xlnt = p.sbuf("xlnt_sb", [128, 16 * NT], BF16)
    acc = p.sbuf("acc_sb", [128, 8 * HID], F32)
    gh = p.sbuf("gh_sb", [128, 44 * 512], BF16)
    gwt_sb = p.sbuf("gwt_sb", [128, 16 * 16], BF16)
    sgv_sb = p.sbuf("sgv_sb", [128, 16], BF16)
    sig_sb = p.sbuf("sig_sb", [128, 8], F32)
    comb = p.sbuf("comb_sb", [128, 8 * 16], F32)
    probs = p.sbuf("probs_sb", [128, 16], F32)
    wrk = p.sbuf("wrk_sb", [128, 16], F32)
    eqt = p.sbuf("eqt_sb", [128, 16], F32)
    mx = p.sbuf("mx_sb", [128, 4], F32)
    s4 = p.sbuf("s4_sb", [128, 1], F32)
    rs4 = p.sbuf("rs4_sb", [128, 1], F32)
    negm = p.sbuf("negm_sb", [128, 1], F32)
    wring = [p.sbuf(f"wring{i}", [128, 16 * 128], BF16) for i in range(2)]
    dring = [p.sbuf(f"dring{i}", [128, 512], BF16) for i in range(2)]
    stmp = p.sbuf("stmp", [128, 512], BF16)
    ytmp = p.sbuf("ytmp", [128, 512], F32)
    lnout = p.sbuf("lnout", [128, HID], F32)
    xo = p.sbuf("xo", [128, HID], F32)
    idn_f = p.sbuf("idn_f", [128, 128], F32)
    tmp = make_ln_tmp(p, HID)
    PS = [p.psum(f"PS{i}", [128, 512], F32) for i in range(8)]

    p.a(V, lambda v: v.memset(tmp["weps"][:], HID * LN_EPS))
    p.a(V, lambda v: v.memset(tmp["lnwh"][:], 0.5 * float(np.log(HID))))
    p.admas(SY, idn_f[:], ins["c_idn_f32"][:, :])
    p.admas(G, gwt_sb[:].rearrange("p (c j) -> p c j", c=16),
            ins["gw_t"][:, :].rearrange("(c p) j -> p c j", p=128))
    p.admas(G, sgv_sb[:], ins["sgv_l"][:, :])

    def mm(out_ap, l_ap, r_ap, start=True, stop=True):
        p.a(T, lambda t, o=out_ap, a=l_ap, b=r_ap, st=start, sp=stop:
            t.matmul(o, a, b, start=st, stop=sp))

    def scp(out_ap, in_ap):
        p.a(S, lambda s, o=out_ap, i=in_ap: s.copy(o, i))

    def act(out_ap, in_ap, f, bias=0.0, scale=1.0, accum=None):
        p.a(S, lambda s, o=out_ap, i=in_ap, ff=f, b=bias, sc=scale, ac=accum:
            s.activation(o, i, ff, bias=b, scale=sc, accum_out=ac)
            if ac is not None else
            s.activation(o, i, ff, bias=b, scale=sc))

    for t in range(8):
        bi = t % 2
        p.w(SY, "vector", "scalar")
        p.admas(SY, xrt[bi][:], ins["xres"][t * 128:(t + 1) * 128, :])
        p.w(V, "dsync")
        p.w(S, "dsync")
        p.w(V, "tensor")
        emit_ln(p, xrt[bi][:], xlnf[:], HID, tmp, p.auto["vector"], p.auto["scalar"])
        p.w(T, "vector", "dsync")
        for c in range(16):
            cb = c % 2
            p.w(T, "scalar")
            p.a(T, lambda tt, c=c, cb=cb: tt.transpose(
                PS[2][:, cb * 128:cb * 128 + 128], xlnf[:, c * 128:(c + 1) * 128],
                idn_f[:]))
            p.w(S, "tensor")
            scp(xlnt[:, c * NT + t * 128: c * NT + (t + 1) * 128],
                PS[2][:, cb * 128:cb * 128 + 128])
        p.w(T, "scalar", "dgpsimd")
        for c in range(16):
            mm(PS[0][:, 0:16], xlnt[:, c * NT + t * 128: c * NT + (t + 1) * 128],
               gwt_sb[:, c * 16:(c + 1) * 16], start=(c == 0), stop=(c == 15))
        p.w(V, "tensor")
        p.a(V, lambda v: v.reduce_max(negm[:], PS[0][:, 0:16], axis=AX.X, negate=True))
        p.w(S, "vector")
        act(probs[:], PS[0][:, 0:16], AF.Exp, bias=negm[:])
        p.w(V, "scalar")
        p.a(V, lambda v: v.tensor_copy(wrk[:], probs[:]))
        for jj in range(4):
            p.w(V, "self")
            p.a(V, lambda v, jj=jj: v.reduce_max(mx[:, jj:jj + 1], wrk[:], axis=AX.X))
            if jj < 3:
                p.w(V, "self")
                p.a(V, lambda v, jj=jj: v.tensor_scalar(
                    eqt[:], wrk[:], mx[:, jj:jj + 1], None, ALU.is_ge))
                p.w(V, "self")
                p.a(V, lambda v: v.scalar_tensor_tensor(
                    wrk[:], eqt[:], -1e30, wrk[:], ALU.mult, ALU.add))
        p.w(V, "self")
        p.a(V, lambda v: v.tensor_scalar(eqt[:], probs[:], mx[:, 3:4], None, ALU.is_ge))
        p.w(V, "self")
        p.a(V, lambda v: v.tensor_mul(wrk[:], probs[:], eqt[:]))
        p.w(V, "self")
        p.a(V, lambda v: v.reduce_sum(s4[:], wrk[:], axis=AX.X))
        p.w(V, "self")
        p.a(V, lambda v: v.reciprocal(rs4[:], s4[:]))
        p.w(V, "self")
        p.a(V, lambda v, t=t: v.tensor_scalar(
            comb[:, t * 16:(t + 1) * 16], wrk[:], rs4[:], None, ALU.mult))
        p.w(T, "vector")
        for c in range(16):
            mm(PS[1][:, 0:1], xlnt[:, c * NT + t * 128: c * NT + (t + 1) * 128],
               sgv_sb[:, c:c + 1], start=(c == 0), stop=(c == 15))
        p.w(S, "tensor")
        act(sig_sb[:, t:t + 1], PS[1][:, 0:1], AF.Sigmoid)

    for blk in range(2):
        tok0 = blk * 512
        for m in range(44):
            wr = wring[m % 2]
            p.w(G, "tensor")
            p.admas(G, wr[:].rearrange("p (c j) -> p c j", c=16),
                    ins["sgu_t"][:, m * 128:(m + 1) * 128]
                    .rearrange("(c p) j -> p c j", p=128))
            p.w(T, "dgpsimd", "scalar")
            for c in range(16):
                mm(PS[0][:, 0:512], wr[:, c * 128:(c + 1) * 128],
                   xlnt[:, c * NT + tok0: c * NT + tok0 + 512],
                   start=(c == 0), stop=(c == 15))
            p.w(S, "tensor")
            scp(gh[:, m * 512:(m + 1) * 512], PS[0][:, 0:512])
        for m in range(44):
            wr = wring[m % 2]
            p.w(G, "tensor")
            p.admas(G, wr[:].rearrange("p (c j) -> p c j", c=16),
                    ins["sgu_t"][:, (44 + m) * 128:(45 + m) * 128]
                    .rearrange("(c p) j -> p c j", p=128))
            p.w(T, "dgpsimd", "vector")
            for c in range(16):
                mm(PS[0][:, 0:512], wr[:, c * 128:(c + 1) * 128],
                   xlnt[:, c * NT + tok0: c * NT + tok0 + 512],
                   start=(c == 0), stop=(c == 15))
            p.w(S, "tensor", "vector")
            act(stmp[:], gh[:, m * 512:(m + 1) * 512], AF.Silu)
            p.w(V, "scalar", "tensor")
            p.a(V, lambda v, m=m: v.tensor_tensor(
                gh[:, m * 512:(m + 1) * 512], stmp[:], PS[0][:, 0:512], ALU.mult))
        for n in range(4):
            p.w(T, "scalar", "vector")
            for k in range(44):
                dr = dring[k % 2]
                p.w(G, "tensor")
                p.admas(G, dr[:], ins["sdn_t"][k * 128:(k + 1) * 128,
                                              n * 512:(n + 1) * 512])
                p.w(T, "dgpsimd")
                for ts in range(4):
                    mm(PS[4 + ts][:, 0:512],
                       gh[:, k * 512 + ts * 128: k * 512 + (ts + 1) * 128],
                       dr[:], start=(k == 0), stop=(k == 43))
            p.w(S, "tensor")
            for ts in range(4):
                gts = blk * 4 + ts
                act(acc[:, gts * HID + n * 512: gts * HID + n * 512 + 512],
                    PS[4 + ts][:, 0:512], AF.Copy, scale=sig_sb[:, gts:gts + 1])

    for e in range(nexp):
        for m in range(11):
            wr = wring[m % 2]
            p.w(G, "tensor")
            p.admas(G, wr[:].rearrange("p (c j) -> p c j", c=16),
                    ins["egu_t"][e, :, m * 128:(m + 1) * 128]
                    .rearrange("(c p) j -> p c j", p=128))
            p.w(T, "dgpsimd", "scalar")
            for blk in range(2):
                for c in range(16):
                    mm(PS[blk][:, 0:512], wr[:, c * 128:(c + 1) * 128],
                       xlnt[:, c * NT + blk * 512: c * NT + blk * 512 + 512],
                       start=(c == 0), stop=(c == 15))
            p.w(S, "tensor")
            scp(gh[:, m * NT: m * NT + 512], PS[0][:, 0:512])
            scp(gh[:, m * NT + 512: (m + 1) * NT], PS[1][:, 0:512])
        for m in range(11):
            wr = wring[m % 2]
            p.w(G, "tensor")
            p.admas(G, wr[:].rearrange("p (c j) -> p c j", c=16),
                    ins["egu_t"][e, :, (11 + m) * 128:(12 + m) * 128]
                    .rearrange("(c p) j -> p c j", p=128))
            p.w(T, "dgpsimd", "vector")
            for blk in range(2):
                for c in range(16):
                    mm(PS[blk][:, 0:512], wr[:, c * 128:(c + 1) * 128],
                       xlnt[:, c * NT + blk * 512: c * NT + blk * 512 + 512],
                       start=(c == 0), stop=(c == 15))
            for blk in range(2):
                p.w(S, "tensor", "vector")
                act(stmp[:], gh[:, m * NT + blk * 512: m * NT + blk * 512 + 512],
                    AF.Silu)
                p.w(V, "scalar")
                p.a(V, lambda v, m=m, blk=blk: v.tensor_tensor(
                    gh[:, m * NT + blk * 512: m * NT + blk * 512 + 512],
                    stmp[:], PS[blk][:, 0:512], ALU.mult))
        for n in range(4):
            p.w(T, "scalar", "vector")
            for k in range(11):
                dr = dring[k % 2]
                p.w(G, "tensor")
                p.admas(G, dr[:], ins["edn_t"][e, k * 128:(k + 1) * 128,
                                              n * 512:(n + 1) * 512])
                p.w(T, "dgpsimd")
                for ts in range(8):
                    mm(PS[ts][:, 0:512],
                       gh[:, k * NT + ts * 128: k * NT + (ts + 1) * 128],
                       dr[:], start=(k == 0), stop=(k == 10))
            p.w(S, "tensor", "vector")
            for ts in range(8):
                act(ytmp[:], PS[ts][:, 0:512], AF.Copy,
                    scale=comb[:, ts * 16 + e: ts * 16 + e + 1])
                p.w(V, "scalar")
                dst = acc[:, ts * HID + n * 512: ts * HID + n * 512 + 512]
                p.a(V, lambda v, dst=dst: v.tensor_tensor(dst, dst, ytmp[:], ALU.add))
                p.w(S, "vector")
            p.w(V, "self")

    for t in range(8):
        bi = t % 2
        p.w(SY, "vector", "scalar")
        p.admas(SY, xrt[bi][:], ins["xres"][t * 128:(t + 1) * 128, :])
        p.w(V, "dsync", "self")
        p.w(S, "dsync")
        emit_ln(p, acc[:, t * HID:(t + 1) * HID], lnout[:], HID, tmp,
                p.auto["vector"], p.auto["scalar"])
        p.w(V, "self", "dsync")
        p.a(V, lambda v, bi=bi: v.tensor_add(xo[:], xrt[bi][:], lnout[:]))
        p.w(SY, "vector")
        p.admas(SY, xout[t * 128:(t + 1) * 128, :], xo[:])
    p.wait(SY, p.adma[SY], p.adma[SY].n)
    p.finalize()
    return nc



def build_l1_attn(nc, nheads=8):
    SEQ, HID = 2048, 2048
    ins = declare_inputs(nc, {
        "xlnt": ([HID, SEQ], BF16),
        "wq": ([HID, 1024], F32),
        "wk": ([HID, 1024], F32),
        "wv": ([HID, 1024], F32),
        "bq_l": ([128, 8], F32),
        "bk_l": ([128, 8], F32),
        "bv_l": ([128, 8], F32),
        "alibi_c": ([128, 8 * 16], F32),
        "negbd": ([8, SEQ], F32),
        "c_idn": ([128, 128], BF16),
        "c_ones_row_bf": ([1, 128], BF16),
        "c_ones_row": ([1, 128], F32),
        "c_ones_col_bf": ([128, 1], BF16),
        "c_bmask0": ([128, 512], BF16),
        "c_bmask1": ([128, 512], BF16),
        "c_bmask2": ([128, 512], BF16),
        "c_bmask3": ([128, 512], BF16),
    })
    ctx_o = nc.dram_tensor("ctx", [1024, SEQ], BF16, kind="ExternalOutput")
    p = Prog(nc)
    p.mk_auto()
    T, V, S, G, SY = "tensor", "vector", "scalar", "gpsimd", "sync"

    xl = p.sbuf("xl", [128, 16 * SEQ], BF16)
    w_sb = [p.sbuf(f"w_sb{i}", [128, 16 * 128], BF16) for i in range(3)]
    q_fm = p.sbuf("q_fm", [128, SEQ], BF16)
    k_fm = p.sbuf("k_fm", [128, SEQ], BF16)
    v_fm = p.sbuf("v_fm", [128, SEQ], BF16)
    v_tm = p.sbuf("v_tm", [128, SEQ], BF16)
    qk_el = p.sbuf("qk_el", [128, SEQ], BF16)
    rrow_f = p.sbuf("rrow_f", [1, 512], F32)
    r_bf = p.sbuf("r_bf", [1, SEQ], BF16)
    nbd = p.sbuf("nbd", [1, SEQ], F32)
    bqc = p.sbuf("bqc", [128, 8], F32)
    bkc = p.sbuf("bkc", [128, 8], F32)
    bvc = p.sbuf("bvc", [128, 8], F32)
    alibi = p.sbuf("alibi", [128, 8 * 16], F32)
    ptile = [p.sbuf(f"ptile{i}", [128, 512], BF16) for i in range(2)]
    rr = p.sbuf("rr", [1, 512], F32)
    rbc = p.sbuf("rbc", [128, 512], F32)
    obase = p.sbuf("obase", [128, SEQ], BF16)
    idn_bf = p.sbuf("idn_bf", [128, 128], BF16)
    ones_rb = p.sbuf("ones_rb", [1, 128], BF16)
    ones_r = p.sbuf("ones_r", [1, 128], F32)
    ones_cb = p.sbuf("ones_cb", [128, 1], BF16)
    bmask = [p.sbuf(f"bmask{r}", [128, 512], BF16) for r in range(4)]
    PS = [p.psum(f"PS{i}", [128, 512], F32) for i in range(6)]
    PTB = [p.psum(f"PTB{i}", [128, 128], BF16) for i in range(2)]

    for nm, dst in [("c_idn", idn_bf), ("c_ones_row_bf", ones_rb),
                    ("c_ones_row", ones_r), ("c_ones_col_bf", ones_cb),
                    ("bq_l", bqc), ("bk_l", bkc), ("bv_l", bvc),
                    ("alibi_c", alibi)]:
        p.admas(SY, dst[:], ins[nm][:, :])
    for r in range(4):
        p.admas(SY, bmask[r][:], ins[f"c_bmask{r}"][:, :])
    for c in range(16):
        p.admas(SY, xl[:, c * SEQ:(c + 1) * SEQ], ins["xlnt"][c * 128:(c + 1) * 128, :])

    def mm(out_ap, l_ap, r_ap, start=True, stop=True):
        p.a(T, lambda t, o=out_ap, a=l_ap, b=r_ap, st=start, sp=stop:
            t.matmul(o, a, b, start=st, stop=sp))

    def act(out_ap, in_ap, f, bias=0.0, scale=1.0):
        p.a(S, lambda s, o=out_ap, i=in_ap, ff=f, b=bias, sc=scale:
            s.activation(o, i, ff, bias=b, scale=sc))

    for hl in range(nheads):
        p.w(G, "tensor")
        for wi, wn in enumerate(["wq", "wk", "wv"]):
            p.admas(G, w_sb[wi][:].rearrange("p (c j) -> p c j", c=16),
                    ins[wn][:, hl * 128:(hl + 1) * 128]
                    .rearrange("(c p) j -> p c j", p=128))
        p.admas(SY, nbd[0:1, :], ins["negbd"][hl:hl + 1, :])
        p.w(T, "dgpsimd", "dsync")
        for wi, dst, bc in [(0, q_fm, bqc), (1, k_fm, bkc), (2, v_fm, bvc)]:
            for n in range(4):
                p.w(T, "scalar", "vector")
                for c in range(16):
                    mm(PS[0][:, 0:512], w_sb[wi][:, c * 128:(c + 1) * 128],
                       xl[:, c * SEQ + n * 512: c * SEQ + n * 512 + 512],
                       start=(c == 0), stop=(c == 15))
                p.w(S, "tensor")
                p.a(S, lambda s, dsl=dst[:, n * 512:(n + 1) * 512], bcc=bc[:, hl:hl + 1]:
                    s.add(dsl, PS[0][:, 0:512], bcc))
        for c in range(16):
            p.w(T, "scalar")
            p.a(T, lambda t, c=c: t.transpose(
                PTB[c % 2][:], v_fm[:, c * 128:(c + 1) * 128], idn_bf[:]))
            p.w(S, "tensor")
            p.a(S, lambda s, c=c: s.copy(v_tm[:, c * 128:(c + 1) * 128],
                                         PTB[c % 2][:]))
        p.w(V, "scalar")
        p.a(V, lambda v: v.tensor_mul(qk_el[:], q_fm[:], k_fm[:]))
        for n in range(4):
            p.w(T, "vector", "scalar")
            mm(PS[2][0:1, 0:512], ones_cb[:, 0:1], qk_el[:, n * 512:(n + 1) * 512])
            p.w(V, "tensor", "dsync")
            p.a(V, lambda v, n=n: v.scalar_tensor_tensor(
                rrow_f[0:1, :], PS[2][0:1, 0:512], -1.0, nbd[0:1, n * 512:(n + 1) * 512],
                ALU.mult, ALU.add))
            p.w(S, "vector")
            p.a(S, lambda s, n=n: s.copy(r_bf[0:1, n * 512:(n + 1) * 512],
                                         rrow_f[0:1, :]))
        p.w(V, "dsync")
        for qb in range(4):
            nkc = (qb + 1) * 4
            for kc in range(nkc):
                sb = 4 + (kc % 2)
                p.w(T, "scalar", "vector")
                mm(PS[sb][:, 0:512], k_fm[:, kc * 128:(kc + 1) * 128],
                   q_fm[:, qb * 512:(qb + 1) * 512], start=True, stop=False)
                mm(PS[sb][:, 0:512], ones_rb[0:1, :], r_bf[0:1, qb * 512:(qb + 1) * 512],
                   start=False, stop=(kc < qb * 4))
                if kc >= qb * 4:
                    mm(PS[sb][:, 0:512], idn_bf[:], bmask[kc - qb * 4][:],
                       start=False, stop=True)
                p.w(S, "tensor")
                act(ptile[kc % 2][:], PS[sb][:, 0:512], AF.Exp,
                    bias=alibi[:, hl * 16 + kc: hl * 16 + kc + 1], scale=DSCALE)
                p.w(T, "scalar")
                mm(PS[2][0:1, 0:512], ones_cb[:, 0:1], ptile[kc % 2][:],
                   start=(kc == 0), stop=(kc == nkc - 1))
                mm(PS[3][:, 0:512], v_tm[:, kc * 128:(kc + 1) * 128],
                   ptile[kc % 2][:], start=(kc == 0), stop=(kc == nkc - 1))
            p.w(V, "tensor")
            p.a(V, lambda v: v.reciprocal(rr[0:1, 0:512], PS[2][0:1, 0:512]))
            p.w(T, "vector")
            mm(PS[1][:, 0:512], ones_r[0:1, :], rr[0:1, 0:512])
            p.w(S, "tensor")
            p.a(S, lambda s: s.copy(rbc[:], PS[1][:, 0:512]))
            p.w(V, "scalar")
            p.a(V, lambda v, qb=qb: v.tensor_tensor(
                obase[:, qb * 512:(qb + 1) * 512], PS[3][:, 0:512], rbc[:], ALU.mult))
        p.w(SY, "vector")
        p.admas(SY, ctx_o[hl * 128:(hl + 1) * 128, :], obase[:])
    p.wait(SY, p.adma[SY], p.adma[SY].n)
    p.finalize()
    return nc



def build_l1ffn(nc):
    NT, HID = 1024, 2048
    ins = declare_inputs(nc, {
        "xres": ([NT, HID], F32),
        "w1t": ([HID, HID], F32),
        "scale_r": ([1, HID], F32),
        "bias_r": ([1, HID], F32),
        "c_idn_f32": ([128, 128], F32),
    })
    xout = nc.dram_tensor("xout", [NT, HID], F32, kind="ExternalOutput")
    p = Prog(nc)
    p.mk_auto()
    T, V, S, G, SY = "tensor", "vector", "scalar", "gpsimd", "sync"
    w_sb = p.sbuf("w_sb", [128, 16 * HID], BF16)
    xrt = [p.sbuf(f"xrt{i}", [128, HID], F32) for i in range(2)]
    xlnf = p.sbuf("xlnf", [128, HID], F32)
    xlnt = p.sbuf("xlnt_sb", [128, 16 * 128], BF16)
    sc_bc = p.sbuf("sc_bc", [128, HID], F32)
    bi_bc = p.sbuf("bi_bc", [128, HID], F32)
    mixed = p.sbuf("mixed", [128, HID], F32)
    y1 = p.sbuf("y1", [128, 512], F32)
    lnout = p.sbuf("lnout", [128, HID], F32)
    xo = p.sbuf("xo", [128, HID], F32)
    idn_f = p.sbuf("idn_f", [128, 128], F32)
    tmp = make_ln_tmp(p, HID)
    tmp2 = make_ln_tmp(p, HID, tag="b")
    PS = [p.psum(f"PS{i}", [128, 512], F32) for i in range(3)]

    p.a(V, lambda v: v.memset(tmp["weps"][:], HID * LN_EPS))
    p.a(V, lambda v: v.memset(tmp["lnwh"][:], 0.5 * float(np.log(HID))))
    p.a(V, lambda v: v.memset(tmp2["weps"][:], HID * LN_EPS))
    p.a(V, lambda v: v.memset(tmp2["lnwh"][:], 0.5 * float(np.log(HID))))
    p.admas(SY, idn_f[:], ins["c_idn_f32"][:, :])
    p.admas(SY, sc_bc[:], ins["scale_r"][:, :].broadcast_to([128, HID]))
    p.admas(SY, bi_bc[:], ins["bias_r"][:, :].broadcast_to([128, HID]))
    for c in range(4):
        p.admas(G, w_sb[:].rearrange("p (c j) -> p c j", c=16)[:, 4 * c:4 * c + 4, :],
                ins["w1t"][:, :].rearrange("(c p) j -> p c j", p=128)[:, 4 * c:4 * c + 4, :])

    def mm(out_ap, l_ap, r_ap, start=True, stop=True):
        p.a(T, lambda t, o=out_ap, a=l_ap, b=r_ap, st=start, sp=stop:
            t.matmul(o, a, b, start=st, stop=sp))

    for t in range(8):
        bi = t % 2
        p.w(SY, "vector", "scalar")
        p.admas(SY, xrt[bi][:], ins["xres"][t * 128:(t + 1) * 128, :])
        p.w(V, "dsync", "tensor")
        p.w(S, "dsync")
        emit_ln(p, xrt[bi][:], xlnf[:], HID, tmp, p.auto["vector"], p.auto["scalar"])
        p.w(T, "vector")
        for c in range(16):
            cb = c % 2
            p.w(T, "scalar")
            p.a(T, lambda tt, c=c, cb=cb: tt.transpose(
                PS[2][:, cb * 128:cb * 128 + 128], xlnf[:, c * 128:(c + 1) * 128],
                idn_f[:]))
            p.w(S, "tensor")
            p.a(S, lambda s, c=c, cb=cb: s.copy(
                xlnt[:, c * 128:(c + 1) * 128], PS[2][:, cb * 128:cb * 128 + 128]))
        for n in range(4):
            p.w(T, "scalar", "vector", "dgpsimd")
            for c in range(16):
                mm(PS[0][:, 0:512], xlnt[:, c * 128:(c + 1) * 128],
                   w_sb[:, c * HID + n * 512: c * HID + n * 512 + 512],
                   start=(c == 0), stop=(c == 15))
            p.w(V, "tensor", "dsync")
            p.a(V, lambda v, n=n: v.tensor_tensor(
                y1[:], PS[0][:, 0:512], sc_bc[:, n * 512:(n + 1) * 512], ALU.mult))
            p.w(V, "self")
            p.a(V, lambda v, n=n: v.tensor_tensor(
                mixed[:, n * 512:(n + 1) * 512], y1[:],
                bi_bc[:, n * 512:(n + 1) * 512], ALU.add))
        p.w(V, "self")
        p.w(S, "vector")
        emit_ln(p, mixed[:], lnout[:], HID, tmp2, p.auto["vector"], p.auto["scalar"])
        p.w(V, "self", "dsync")
        p.a(V, lambda v, bi=bi: v.tensor_add(xo[:], xrt[bi][:], lnout[:]))
        p.w(SY, "vector")
        p.admas(SY, xout[t * 128:(t + 1) * 128, :], xo[:])
    p.wait(SY, p.adma[SY], p.adma[SY].n)
    p.finalize()
    return nc



from concourse.bass_utils import run_bass_kernel_spmd

_CONSTS_L0ATTN = ["c_idn", "c_idn_f32", "c_uincl", "c_ones_row",
                  "c_ones_col_bf", "c_m_sl", "c_m_su", "c_m_iu",
                  "c_bmask0", "c_bmask1", "c_bmask2", "c_bmask3"]
_CONSTS_L1ATTN = ["c_idn", "c_ones_row_bf", "c_ones_row", "c_ones_col_bf",
                  "c_bmask0", "c_bmask1", "c_bmask2", "c_bmask3"]


def _launch(build_fn, maps, *args, **kwargs):
    nc = bass.Bass("TRN2", target_bir_lowering=False)
    build_fn(nc, *args, **kwargs)
    res = run_bass_kernel_spmd(nc, maps, core_ids=list(range(8)))
    return res.results


def _ln_tr_all(C, Xb):
    maps = [{"xin": np.ascontiguousarray(Xb[c // 2][(c % 2) * 1024:
                                                    (c % 2 + 1) * 1024]),
             "c_idn": C["c_idn"]} for c in range(8)]
    r = _launch(build_ln_tr, maps)
    return [np.concatenate([r[2 * b]["xlnt"], r[2 * b + 1]["xlnt"]], axis=1)
            for b in range(4)]


def kernel(**kw):
    C = host_constants()
    f32 = np.float32
    g = {k: np.ascontiguousarray(np.asarray(v, dtype=f32)) for k, v in kw.items()}
    B = 4
    Xb = [g["L0_attn_hidden_states"][b] for b in range(B)]

    xlnT = _ln_tr_all(C, Xb)
    maps = []
    for c in range(8):
        b, hg = c // 2, c % 2
        m = {"xlnt": xlnT[b],
             "wq": np.ascontiguousarray(g["L0_attn_Wq"][:, hg * 1024:(hg + 1) * 1024]),
             "wk": np.ascontiguousarray(g["L0_attn_Wk"][:, hg * 512:(hg + 1) * 512]),
             "wv": np.ascontiguousarray(g["L0_attn_Wv"][:, hg * 512:(hg + 1) * 512]),
             "wg": g["L0_attn_Wg"], "bg": g["L0_attn_bg"].reshape(1, 1)}
        for n in _CONSTS_L0ATTN:
            m[n] = C[n]
        maps.append(m)
    r = _launch(build_l0_attn, maps)
    fla = [[r[2 * b]["fla"], r[2 * b + 1]["fla"]] for b in range(B)]
    base = [[r[2 * b]["baseo"], r[2 * b + 1]["baseo"]] for b in range(B)]
    maps = []
    for c in range(8):
        b, hf = c // 2, c % 2
        sl = slice(hf * 1024, (hf + 1) * 1024)
        maps.append({
            "xres": np.ascontiguousarray(Xb[b][sl]),
            "wo": g["L0_attn_Wo"],
            "fla": np.ascontiguousarray(
                np.concatenate([fla[b][0][sl], fla[b][1][sl]], axis=1)),
            "base": np.ascontiguousarray(
                np.concatenate([base[b][0][:, sl], base[b][1][:, sl]], axis=0)),
        })
    r = _launch(build_mix, maps, "l0")
    X1 = [np.concatenate([r[2 * b]["xout"], r[2 * b + 1]["xout"]], axis=0)
          for b in range(B)]

    egu_t = np.ascontiguousarray(g["L0_ffn_expert_gate_up"].transpose(0, 2, 1))
    edn_t = np.ascontiguousarray(g["L0_ffn_expert_down"].transpose(0, 2, 1))
    sgu_t = np.ascontiguousarray(g["L0_ffn_shared_gate_up"].T)
    sdn_t = np.ascontiguousarray(g["L0_ffn_shared_down"].T)
    sgv_l = np.ascontiguousarray(g["L0_ffn_shared_gate_vec"][0].reshape(16, 128).T)
    gw_t = np.ascontiguousarray(g["L0_ffn_gate_w"].T)
    maps = []
    for c in range(8):
        b, hf = c // 2, c % 2
        maps.append({
            "xres": np.ascontiguousarray(X1[b][hf * 1024:(hf + 1) * 1024]),
            "gw_t": gw_t, "egu_t": egu_t, "edn_t": edn_t, "sgu_t": sgu_t,
            "sdn_t": sdn_t, "sgv_l": sgv_l, "c_idn_f32": C["c_idn_f32"],
        })
    r = _launch(build_moe, maps)
    X2 = [np.concatenate([r[2 * b]["xout"], r[2 * b + 1]["xout"]], axis=0)
          for b in range(B)]

    xlnT = _ln_tr_all(C, X2)
    H = 16
    slopes = (2.0 ** (-8.0 / H)) ** np.arange(1, H + 1)
    dist = np.arange(2048, dtype=np.float64) - 2047.0
    maps = []
    for c in range(8):
        b, hg = c // 2, c % 2
        hsl = slice(hg * 1024, (hg + 1) * 1024)
        alibi_c = np.zeros((128, 8 * 16), np.float32)
        negbd = np.zeros((8, 2048), np.float32)
        for h in range(8):
            gh = hg * 8 + h
            ac = slopes[gh] * dist
            alibi_c[:, h * 16:(h + 1) * 16] = ac.reshape(16, 128).T
            negbd[h] = -(slopes[gh] * dist) / DSCALE
        m = {"xlnt": xlnT[b],
             "wq": np.ascontiguousarray(g["L1_attn_Wq"][:, hsl]),
             "wk": np.ascontiguousarray(g["L1_attn_Wk"][:, hsl]),
             "wv": np.ascontiguousarray(g["L1_attn_Wv"][:, hsl]),
             "bq_l": np.ascontiguousarray(g["L1_attn_bq"][hsl].reshape(8, 128).T),
             "bk_l": np.ascontiguousarray(g["L1_attn_bk"][hsl].reshape(8, 128).T),
             "bv_l": np.ascontiguousarray(g["L1_attn_bv"][hsl].reshape(8, 128).T),
             "alibi_c": alibi_c, "negbd": negbd}
        for n in _CONSTS_L1ATTN:
            m[n] = C[n]
        maps.append(m)
    r = _launch(build_l1_attn, maps)
    ctx = [[r[2 * b]["ctx"], r[2 * b + 1]["ctx"]] for b in range(B)]
    maps = []
    for c in range(8):
        b, hf = c // 2, c % 2
        sl = slice(hf * 1024, (hf + 1) * 1024)
        maps.append({
            "xres": np.ascontiguousarray(X2[b][sl]),
            "wo": g["L1_attn_Wo"],
            "ctx": np.ascontiguousarray(
                np.concatenate([ctx[b][0][:, sl], ctx[b][1][:, sl]], axis=0)),
            "bo": g["L1_attn_bo"].reshape(1, -1),
        })
    r = _launch(build_mix, maps, "l1")
    X3 = [np.concatenate([r[2 * b]["xout"], r[2 * b + 1]["xout"]], axis=0)
          for b in range(B)]

    w1t = np.ascontiguousarray(g["L1_ffn_weight"].T)
    maps = []
    for c in range(8):
        b, hf = c // 2, c % 2
        maps.append({
            "xres": np.ascontiguousarray(X3[b][hf * 1024:(hf + 1) * 1024]),
            "w1t": w1t,
            "scale_r": g["L1_ffn_scale_weight"].reshape(1, -1),
            "bias_r": g["L1_ffn_bias"].reshape(1, -1),
            "c_idn_f32": C["c_idn_f32"],
        })
    r = _launch(build_l1ffn, maps)
    out = np.stack([np.concatenate([r[2 * b]["xout"], r[2 * b + 1]["xout"]], axis=0)
                    for b in range(B)])
    return out.astype(np.float32)
